# version 09 (final)
# baseline (speedup 1.0000x reference)
import numpy as np
import ml_dtypes
import concourse.bass as bass
import concourse.mybir as mybir
from concourse.bass_utils import run_bass_kernel_spmd
from contextlib import ExitStack

F32 = mybir.dt.float32
BF16 = mybir.dt.bfloat16
I32 = mybir.dt.int32
AF = mybir.ActivationFunctionType
ALU = mybir.AluOpType
AX = mybir.AxisListType

H = 32
DM = 4096
SEQ = 2048
TOWN = 1024
CTX = 1536
OOFF = 256
NG = 32
NT = 12
SCALE = 1.0 / float(np.sqrt(np.float32(128)))
ENGINES = ("sync", "tensor", "vector", "scalar", "gpsimd")

WINS = [(0, 256), (0, 512), (256, 768), (512, 1024), (768, 1024)]
WOFF = [0, 256, 768, 1280, 1792]
WNCOLS = 2048
BLENDS = [(0, 0, 1), (1, 256, 2), (2, 256, 3), (3, 256, 4)]


class Plan:

    def __init__(self):
        self.events = []
        self.ctr = {}

    def sem(self, nc, name, ctx):
        s = ctx.enter_context(nc.semaphore(name))
        self.ctr[name] = [s, 0]
        return name

    def on(self, engine, fn):
        self.events.append((engine, fn))

    def wait(self, engine, semname, val=None):
        s, n = self.ctr[semname]
        v = n if val is None else val
        if v <= 0:
            return
        self.events.append((engine, lambda eng, s=s, v=v: eng.wait_ge(s, v)))

    def guard(self, engine, use):
        if use is not None:
            self.wait(engine, use[0], use[1])

    def cur(self, semname):
        return self.ctr[semname][1]

    def use(self, semname):
        return (semname, self.ctr[semname][1])

    def run(self, engine, semname, fn, amt=1):
        s, _ = self.ctr[semname]
        self.events.append((engine, lambda eng, s=s, fn=fn, amt=amt:
                            fn(eng).then_inc(s, amt)))
        self.ctr[semname][1] += amt
        return self.ctr[semname][1]

    def barrier(self):
        for eng in ENGINES:
            for semname in list(self.ctr):
                self.wait(eng, semname)

    def replay(self, engine_name, eng):
        for e, fn in self.events:
            if e == engine_name:
                fn(eng)


def build_nc(debug=False):
    nc = bass.Bass("TRN2", target_bir_lowering=False)

    def di(name, shape, dt):
        return nc.dram_tensor(name, shape, dt, kind="ExternalInput").ap()

    x_ctx = di("x_ctx", [CTX, DM], F32)
    wqT = di("wqT", [DM, DM], F32)
    woT = di("woT", [DM, DM], F32)
    wkT = di("wkT", [DM, 128], F32)
    wvT = di("wvT", [DM, 128], F32)
    qw = di("qw", [DM // 8, DM], I32)
    qzin = di("qz", [NG, DM // 8], I32)
    scl = di("scl", [NG, DM], F32)
    bias_t = di("bias_t", [128, 32], F32)
    cosf = di("cosf", [128, CTX], F32)
    sinf = di("sinf", [128, CTX], F32)
    alp4 = di("alp4", [128, 1024], F32)
    oma4 = di("oma4", [128, 1024], F32)
    shf = di("shf", [128, 1], I32)
    idb = di("idb", [128, 128], BF16)
    idf = di("idf", [128, 128], F32)
    vmk = di("vmk", [128, NT], BF16)
    out_d = nc.dram_tensor("out_s", [TOWN, DM], F32, kind="ExternalOutput").ap()
    if debug:
        dbg = {n: nc.dram_tensor(n, s, dt, kind="ExternalOutput").ap()
               for n, s, dt in [("d_xn", [128, H * TOWN], BF16),
                                ("d_kr", [128, CTX], BF16),
                                ("d_vtok", [128, NT * 129], BF16),
                                ("d_cs", [NG, TOWN], F32),
                                ("d_qr", [128, H * TOWN], BF16),
                                ("d_ao", [128, H * TOWN], BF16),
                                ("d_wn", [128, 2048], F32),
                                ("d_zs", [NG, DM], F32),
                                ("d_w", [256, DM], BF16),
                                ("d_gsl", [128, 2 * TOWN], F32),
                                ("d_dbuf", [128, 1024], F32)]}
    z_scr = nc.dram_tensor("z_scr", [2, 512], F32, kind="Internal").ap()
    w_dram = nc.dram_tensor("w_dram", [DM, DM], BF16, kind="Internal").ap()
    zs_dram = nc.dram_tensor("zs_dram", [NG, DM], F32, kind="Internal").ap()

    p = Plan()
    with ExitStack() as ctx:
        e = ctx.enter_context
        LS = p.sem(nc, "ls", ctx)
        LG = p.sem(nc, "lg", ctx)
        LC = p.sem(nc, "lc", ctx)
        LV = p.sem(nc, "lv", ctx)
        TE = p.sem(nc, "te", ctx)
        VE = p.sem(nc, "ve", ctx)
        SE = p.sem(nc, "se", ctx)
        GE = p.sem(nc, "ge", ctx)
        OUT = p.sem(nc, "outq", ctx)
        WSA = p.sem(nc, "wsa", ctx)
        WSB = p.sem(nc, "wsb", ctx)
        WOA = p.sem(nc, "woa", ctx)
        WOB = p.sem(nc, "wob", ctx)
        WQA = p.sem(nc, "wqa", ctx)
        WQB = p.sem(nc, "wqb", ctx)

        xn_own = e(nc.sbuf_tensor("xn_own", [128, H * TOWN], BF16))
        qr_ao = e(nc.sbuf_tensor("qr_ao", [128, H * TOWN], BF16))
        csb = e(nc.sbuf_tensor("csb", [NG, TOWN], F32))
        biasc = e(nc.sbuf_tensor("biasc", [128, 32], F32))
        identf = e(nc.sbuf_tensor("identf", [128, 128], F32))
        shfc = e(nc.sbuf_tensor("shfc", [128, 1], I32))
        stats = [e(nc.sbuf_tensor(f"st{i}", [128, 1], F32)) for i in range(6)]
        consts = e(nc.sbuf_tensor("consts", [128, 3], F32))
        nc.const_aps.register(F32, 1e-5, consts[:, 0:1])
        nc.const_aps.register(F32, -8.0, consts[:, 1:2])
        nc.const_aps.register(F32, 0.0, consts[:, 2:3])
        p.run("vector", VE, lambda v: v.memset(consts[:, 0:1], 1e-5))
        p.run("vector", VE, lambda v: v.memset(consts[:, 1:2], -8.0))
        p.run("vector", VE, lambda v: v.memset(consts[:, 2:3], 0.0))
        rp_a = e(nc.sbuf_tensor("rp_a", [128, 512], F32))
        rp_b = e(nc.sbuf_tensor("rp_b", [128, 512], F32))
        rp_c = e(nc.sbuf_tensor("rp_c", [128, 512], F32))

        def ld(dst, src):
            p.run("sync", LS, lambda q, d=dst, s=src: q.dma_start(out=d, in_=s), 16)

        ld(biasc[:], bias_t)
        ld(identf[:], idf)
        ld(shfc[:], shf)

        rp_use = [None]

        def rope(ps, dst, dst_col, width, cosc, sinc, pos):
            p.guard("vector", rp_use[0])
            p.run("vector", VE, lambda v: v.tensor_tensor(
                out=rp_a[0:64, 0:width], in0=ps[64:128, 0:width],
                in1=sinc[0:64, pos:pos + width], op=ALU.mult))
            p.run("vector", VE, lambda v: v.tensor_tensor(
                out=rp_b[64:128, 0:width], in0=ps[0:64, 0:width],
                in1=sinc[64:128, pos:pos + width], op=ALU.mult))
            p.run("vector", VE, lambda v: v.tensor_tensor(
                out=rp_c[:, 0:width], in0=ps[:, 0:width],
                in1=cosc[:, pos:pos + width], op=ALU.mult))
            p.wait("gpsimd", VE)
            p.run("gpsimd", GE, lambda g: g.tensor_tensor(
                out=dst[0:64, dst_col:dst_col + width], in0=rp_c[0:64, 0:width],
                in1=rp_a[0:64, 0:width], op=ALU.subtract))
            p.run("gpsimd", GE, lambda g: g.tensor_tensor(
                out=dst[64:128, dst_col:dst_col + width],
                in0=rp_c[64:128, 0:width], in1=rp_b[64:128, 0:width],
                op=ALU.add))
            rp_use[0] = p.use("ge")

        with ExitStack() as ca:
            ea = ca.enter_context
            kr = ea(nc.sbuf_tensor("kr", [128, CTX], BF16))
            vtok = ea(nc.sbuf_tensor("vtok", [128, NT * 129], BF16))
            cosc = ea(nc.sbuf_tensor("cosc", [128, CTX], F32))
            sinc = ea(nc.sbuf_tensor("sinc", [128, CTX], F32))
            alpc = ea(nc.sbuf_tensor("alpc", [128, 1024], F32))
            omac = ea(nc.sbuf_tensor("omac", [128, 1024], F32))
            identb = ea(nc.sbuf_tensor("identb", [128, 128], BF16))
            bmat = ea(nc.sbuf_tensor("bmat", [128, 32 * 32], BF16))
            ld(cosc[:], cosf)
            ld(sinc[:], sinf)
            ld(alpc[:], alp4)
            ld(omac[:], oma4)
            ld(identb[:], idb)
            p.run("vector", VE, lambda v: v.memset(bmat[:], 0.0))
            for g in range(NG):
                p.run("vector", VE, lambda v, g=g: v.memset(
                    bmat[:, 32 * g + g:32 * g + g + 1], 1.0))
            p.run("vector", VE, lambda v: v.memset(vtok[:], 0.0))
            p.wait("vector", LS)
            p.wait("sync", VE)
            p.run("sync", LV, lambda q: q.dma_start(
                out=vtok[:, 128::129], in_=vmk,
                allow_slow_non_contiguous=True), 16)

            with ExitStack() as c0:
                e0 = c0.enter_context
                qz_sb = e0(nc.sbuf_tensor("qz_sb", [NG, DM // 8], I32))
                ziq = e0(nc.sbuf_tensor("ziq", [NG, 1024], I32))
                zmq = e0(nc.sbuf_tensor("zmq", [NG, 1024], F32))
                zsq = e0(nc.sbuf_tensor("zsq", [NG, 1024], F32))
                sclq = e0(nc.sbuf_tensor("sclq", [NG, 1024], F32))
                ld(qz_sb[:], qzin)
                for q4 in range(4):
                    p.run("sync", LS, lambda q, q4=q4: q.dma_start(
                        out=sclq[:], in_=scl[:, 1024 * q4:1024 * (q4 + 1)]), 16)
                    p.wait("vector", LS)
                    for k in range(8):
                        p.run("vector", VE, lambda v, k=k, q4=q4: v.tensor_scalar(
                            out=ziq[:, k::8], in0=qz_sb[:, 128 * q4:128 * (q4 + 1)],
                            scalar1=4 * k, scalar2=15,
                            op0=ALU.logical_shift_right, op1=ALU.bitwise_and))
                    p.wait("scalar", VE)
                    p.run("scalar", SE, lambda s: s.activation(
                        out=zmq[:], in_=ziq[:], func=AF.Copy,
                        bias=-8.0, scale=1.0))
                    p.wait("vector", SE)
                    p.wait("vector", LV)
                    p.run("vector", VE, lambda v: v.scalar_tensor_tensor(
                        out=zsq[:], in0=zmq[:], scalar=-1.0, in1=sclq[:],
                        op0=ALU.mult, op1=ALU.mult))
                    p.wait("sync", VE)
                    p.run("sync", LV, lambda q, q4=q4: q.dma_start(
                        out=zs_dram[:, 1024 * q4:1024 * (q4 + 1)], in_=zsq[:]), 16)
                    p.wait("sync", LV)

            p.barrier()
            with ExitStack() as c0:
                e0 = c0.enter_context
                wk_sb = e0(nc.sbuf_tensor("wk_sb", [128, H * 128], BF16))
                wv_sb = e0(nc.sbuf_tensor("wv_sb", [128, H * 128], BF16))
                vf = e0(nc.sbuf_tensor("vf", [128, CTX], BF16))
                xt = e0(nc.sbuf_tensor("xt", [128, DM], F32))
                ps_t = [e0(nc.psum_tensor(f"ps_t{i}", [128, 128], BF16))
                        for i in range(2)]
                ps_k = e0(nc.psum_tensor("ps_k", [128, 128], F32))
                ps_v = e0(nc.psum_tensor("ps_v", [128, 128], F32))
                ps_cs = e0(nc.psum_tensor("ps_cs", [NG, TOWN], F32))
                xnt = qr_ao[:, 0:DM]
                hfm = qr_ao[:, DM:2 * DM]
                sqtrash = qr_ao[:, 2 * DM:3 * DM]

                for i in range(H):
                    p.run("gpsimd", LG, lambda q, i=i: q.dma_start(
                        out=wk_sb[:, 128 * i:128 * (i + 1)],
                        in_=wkT[128 * i:128 * (i + 1), :]), 16)
                    p.run("gpsimd", LG, lambda q, i=i: q.dma_start(
                        out=wv_sb[:, 128 * i:128 * (i + 1)],
                        in_=wvT[128 * i:128 * (i + 1), :]), 16)

                sums = e0(nc.sbuf_tensor("sums", [128, NT], F32))
                sqs = e0(nc.sbuf_tensor("sqs", [128, NT], F32))
                meanw = e0(nc.sbuf_tensor("meanw", [128, NT], F32))
                m2w = e0(nc.sbuf_tensor("m2w", [128, NT], F32))
                msqw = e0(nc.sbuf_tensor("msqw", [128, NT], F32))
                varw = e0(nc.sbuf_tensor("varw", [128, NT], F32))
                sqvw = e0(nc.sbuf_tensor("sqvw", [128, NT], F32))
                rstdw = e0(nc.sbuf_tensor("rstdw", [128, NT], F32))
                nmrw = e0(nc.sbuf_tensor("nmrw", [128, NT], F32))
                xa_use = None
                p.wait("tensor", LS)
                for n in range(NT):
                    crow = 128 * n
                    p.guard("sync", xa_use)
                    p.run("sync", LS, lambda q, crow=crow: q.dma_start(
                        out=xt[:], in_=x_ctx[crow:crow + 128, :]), 16)
                    p.wait("vector", LS)
                    p.run("vector", VE, lambda v, n=n: v.reduce_sum(
                        out=sums[:, n:n + 1], in_=xt[:], axis=AX.X))
                    p.wait("scalar", LS)
                    p.run("scalar", SE, lambda s, n=n: s.activation(
                        out=sqtrash, in_=xt[:], func=AF.Square,
                        accum_out=sqs[:, n:n + 1]))
                    xa_use = p.use("se")
                    p.wait("sync", VE)
                p.wait("vector", SE)
                p.run("vector", VE, lambda v: v.tensor_scalar(
                    out=msqw[:], in0=sqs[:], scalar1=1.0 / DM,
                    scalar2=None, op0=ALU.mult))
                p.wait("scalar", VE)
                p.run("scalar", SE, lambda s: s.activation(
                    out=meanw[:], in_=sums[:], func=AF.Copy,
                    bias=0.0, scale=-1.0 / DM))
                p.wait("vector", SE)
                p.run("vector", VE, lambda v: v.tensor_tensor(
                    out=m2w[:], in0=meanw[:], in1=meanw[:], op=ALU.mult))
                p.wait("gpsimd", VE)
                p.run("gpsimd", GE, lambda g: g.tensor_tensor(
                    out=varw[:], in0=msqw[:], in1=m2w[:], op=ALU.subtract))
                p.wait("scalar", GE)
                p.run("scalar", SE, lambda s: s.activation(
                    out=sqvw[:], in_=varw[:], func=AF.Sqrt,
                    bias=1e-5, scale=1.0))
                p.wait("vector", SE)
                p.run("vector", VE, lambda v: v.reciprocal(
                    out=rstdw[:], in_=sqvw[:]))
                p.wait("gpsimd", VE)
                p.run("gpsimd", GE, lambda g: g.tensor_tensor(
                    out=nmrw[:], in0=meanw[:], in1=rstdw[:], op=ALU.mult))
                xt_use = None
                xnt_use = None
                tp_use = [None, None]
                kv_started = False
                p.wait("scalar", GE)
                p.wait("scalar", VE)
                for n in range(NT):
                    crow = 128 * n
                    is_own = 2 <= n < 10
                    p.guard("sync", xt_use)
                    p.run("sync", LS, lambda q, crow=crow: q.dma_start(
                        out=xt[:], in_=x_ctx[crow:crow + 128, :]), 16)
                    p.wait("scalar", LS)
                    p.guard("scalar", xnt_use)
                    p.run("scalar", SE, lambda s, n=n: s.activation(
                        out=xnt, in_=xt[:], func=AF.Identity,
                        bias=nmrw[:, n:n + 1], scale=rstdw[:, n:n + 1]))
                    xt_use = p.use("se")
                    p.wait("tensor", SE)
                    for i in range(H):
                        tsl = i % 2
                        p.guard("tensor", tp_use[tsl])
                        p.run("tensor", TE, lambda t, i=i, tsl=tsl: t.transpose(
                            ps_t[tsl][:], xnt[:, 128 * i:128 * (i + 1)],
                            identb[:]))
                        p.wait("vector", TE)
                        if is_own:
                            dc = TOWN * i + 128 * (n - 2)
                            p.run("vector", VE, lambda v, tsl=tsl, dc=dc:
                                  v.tensor_copy(out=xn_own[:, dc:dc + 128],
                                                in_=ps_t[tsl][:]))
                        else:
                            p.run("vector", VE, lambda v, tsl=tsl, i=i:
                                  v.tensor_copy(out=hfm[:, 128 * i:128 * (i + 1)],
                                                in_=ps_t[tsl][:]))
                        tp_use[tsl] = p.use("ve")
                    xnt_use = p.use("te")
                    if not kv_started:
                        p.wait("tensor", LG)
                        kv_started = True
                    p.wait("tensor", VE)
                    for which, wmat, pps in (("k", wk_sb, ps_k),
                                             ("v", wv_sb, ps_v)):
                        for i in range(H):
                            if is_own:
                                rhs_ap = xn_own[:, TOWN * i + 128 * (n - 2):
                                                TOWN * i + 128 * (n - 2) + 128]
                            else:
                                rhs_ap = hfm[:, 128 * i:128 * (i + 1)]
                            p.run("tensor", TE, lambda t, wmat=wmat, pps=pps,
                                  i=i, rhs_ap=rhs_ap: t.matmul(
                                      pps[:], wmat[:, 128 * i:128 * (i + 1)],
                                      rhs_ap, start=(i == 0), stop=(i == H - 1)))
                        p.wait("vector", TE)
                        if which == "k":
                            rope(ps_k, kr, crow, 128, cosc, sinc, crow)
                        else:
                            p.run("vector", VE, lambda v, crow=crow:
                                  v.tensor_copy(out=vf[:, crow:crow + 128],
                                                in_=ps_v[:]))
                p.wait("tensor", VE)
                for kt in range(NT):
                    tsl = kt % 2
                    p.guard("tensor", tp_use[tsl])
                    p.run("tensor", TE, lambda t, kt=kt, tsl=tsl: t.transpose(
                        ps_t[tsl][:], vf[:, 128 * kt:128 * (kt + 1)], identb[:]))
                    p.wait("vector", TE)
                    p.run("vector", VE, lambda v, kt=kt, tsl=tsl: v.tensor_copy(
                        out=vtok[:, 129 * kt:129 * kt + 128], in_=ps_t[tsl][:]))
                    tp_use[tsl] = p.use("ve")
                for st in range(2):
                    for g in range(NG):
                        p.run("tensor", TE, lambda t, st=st, g=g: t.matmul(
                            ps_cs[:, 512 * st:512 * (st + 1)],
                            bmat[:, 32 * g:32 * (g + 1)],
                            xn_own[:, TOWN * g + 512 * st:
                                   TOWN * g + 512 * (st + 1)],
                            start=(g == 0), stop=(g == NG - 1)))
                p.wait("vector", TE)
                p.run("vector", VE, lambda v: v.tensor_copy(
                    out=csb[:], in_=ps_cs[:]))

            p.barrier()
            with ExitStack() as c1:
                e1 = c1.enter_context
                qraw = e1(nc.sbuf_tensor("qraw", [128, 2 * 1024], I32))
                qint = e1(nc.sbuf_tensor("qint", [128, 2 * 1024], I32))
                qm8f = e1(nc.sbuf_tensor("qm8f", [128, 2 * 1024], F32))
                sbc = e1(nc.sbuf_tensor("sbc", [128, 2 * 1024], F32))
                wtl = e1(nc.sbuf_tensor("wtl", [128, 2 * 1024], BF16))
                qr_use = [None, None]
                qi_use = [None, None]
                qm_use = [None, None]
                sb_use = [None, None]
                wt_use = [None, None]
                for it in range(128):
                    g, q4 = divmod(it, 4)
                    sl = it % 2
                    c0_, c1_ = 1024 * sl, 1024 * (sl + 1)
                    jlo = 1024 * q4
                    qq = "sync" if (it % 2 == 0) else "gpsimd"
                    qsem = LS if qq == "sync" else LG
                    p.guard(qq, qr_use[sl])
                    p.run(qq, qsem, lambda q, sl=sl, g=g, jlo=jlo,
                          c0_=c0_, c1_=c1_: q.dma_start(
                              out=qraw[:, c0_:c1_],
                              in_=qw[16 * g:16 * (g + 1), jlo:jlo + 1024]
                              .unsqueeze(1).broadcast_to([16, 8, 1024])), 16)
                    p.guard("scalar", sb_use[sl])
                    p.run("scalar", LC, lambda q, sl=sl, g=g, jlo=jlo,
                          c0_=c0_, c1_=c1_: q.dma_start(
                              out=sbc[:, c0_:c1_],
                              in_=scl[g:g + 1, jlo:jlo + 1024]
                              .broadcast_to([128, 1024])), 16)
                    p.wait("vector", LS if qq == "sync" else LG)
                    p.guard("vector", qi_use[sl])
                    p.run("vector", VE, lambda v, c0_=c0_, c1_=c1_:
                          v.tensor_scalar(out=qint[:, c0_:c1_],
                                          in0=qraw[:, c0_:c1_],
                                          scalar1=shfc[:, 0:1], scalar2=15,
                                          op0=ALU.logical_shift_right,
                                          op1=ALU.bitwise_and))
                    qr_use[sl] = p.use("ve")
                    p.wait("scalar", VE)
                    p.guard("scalar", qm_use[sl])
                    p.run("scalar", SE, lambda s, c0_=c0_, c1_=c1_:
                          s.activation(out=qm8f[:, c0_:c1_],
                                       in_=qint[:, c0_:c1_], func=AF.Copy,
                                       bias=-8.0, scale=1.0))
                    qi_use[sl] = p.use("se")
                    p.wait("vector", SE)
                    p.wait("vector", LC)
                    p.guard("vector", wt_use[sl])
                    p.run("vector", VE, lambda v, c0_=c0_, c1_=c1_:
                          v.tensor_tensor(out=wtl[:, c0_:c1_],
                                          in0=qm8f[:, c0_:c1_],
                                          in1=sbc[:, c0_:c1_], op=ALU.mult))
                    qm_use[sl] = p.use("ve")
                    sb_use[sl] = p.use("ve")
                    p.wait("gpsimd", VE)
                    p.run("gpsimd", LV, lambda q, g=g, jlo=jlo, c0_=c0_,
                          c1_=c1_: q.dma_start(
                              out=w_dram[128 * g:128 * (g + 1), jlo:jlo + 1024],
                              in_=wtl[:, c0_:c1_]), 16)
                    wt_use[sl] = ("lv", p.cur("lv"))

            p.barrier()
            with ExitStack() as c2:
                e2 = c2.enter_context
                wq_sl = e2(nc.sbuf_tensor("wq_sl", [128, H * 256], BF16))
                ps_q = [e2(nc.psum_tensor(f"ps_q{i}", [128, 512], F32))
                        for i in range(8)]
                psq_use = [None] * 8
                wq_use = None
                for o16 in range(16):
                    p.guard("gpsimd", wq_use)
                    wq_cnt = []
                    for i in range(H):
                        p.run("gpsimd", LG, lambda q, i=i, o16=o16: q.dma_start(
                            out=wq_sl[:, 256 * i:256 * (i + 1)],
                            in_=wqT[128 * i:128 * (i + 1),
                                    256 * o16:256 * (o16 + 1)]), 16)
                        wq_cnt.append(p.cur("lg"))
                    base = 4 * (o16 % 2)
                    for k in range(4):
                        p.guard("tensor", psq_use[base + k])
                    p.wait("tensor", LG)
                    for i in range(H):
                        for h2 in range(2):
                            for st in range(2):
                                p.run("tensor", TE, lambda t, i=i, h2=h2,
                                      st=st, base=base: t.matmul(
                                          ps_q[base + 2 * h2 + st][:],
                                          wq_sl[:, 256 * i + 128 * h2:
                                                256 * i + 128 * (h2 + 1)],
                                          xn_own[:, TOWN * i + 512 * st:
                                                 TOWN * i + 512 * (st + 1)],
                                          start=(i == 0), stop=(i == H - 1),
                                          skip_group_check=True))
                    wq_use = p.use("te")
                    p.wait("vector", TE)
                    for h2 in range(2):
                        h = 2 * o16 + h2
                        for st in range(2):
                            rope(ps_q[base + 2 * h2 + st], qr_ao,
                                 TOWN * h + 512 * st, 512, cosc, sinc,
                                 OOFF + 512 * st)
                            psq_use[base + 2 * h2 + st] = p.use("ve")

            p.barrier()
            with ExitStack() as c3:
                e3 = c3.enter_context
                attnT = e3(nc.sbuf_tensor("attnT", [128, 8 * 512], BF16))
                wn = e3(nc.sbuf_tensor("wn", [128, WNCOLS], F32))
                zbc = e3(nc.sbuf_tensor("zbc", [128, 2 * 512], F32))
                zbcr = e3(nc.sbuf_tensor("zbcr", [1, 1024], F32))
                zrow = e3(nc.sbuf_tensor("zrow", [1, 1024], F32))
                btmp = e3(nc.sbuf_tensor("btmp", [128, 1024], F32))
                bt_use = [None, None]
                ps_sc = [e3(nc.psum_tensor(f"ps_sc{i}", [128, 512], F32))
                         for i in range(2)]
                ps_w = [e3(nc.psum_tensor(f"ps_w{i}", [128, 512], F32))
                        for i in range(2)]
                ps_z = [e3(nc.psum_tensor(f"ps_z{i}", [1, 512], F32))
                        for i in range(2)]
                sc_use = [None, None]
                at_use = [None] * 8
                pw_use = [None, None]
                pz_use = [None, None]
                zrow_use = [None, None]
                wcount = [0]
                p.wait("tensor", VE)
                p.wait("tensor", GE)
                p.wait("tensor", LV, 16)
                if debug:
                    p.wait("sync", VE)
                    p.wait("sync", LV)
                    for nm, src in [("d_xn", xn_own[:]), ("d_kr", kr[:]),
                                    ("d_vtok", vtok[:]), ("d_cs", csb[:]),
                                    ("d_qr", qr_ao[:])]:
                        p.run("sync", OUT, lambda q, nm=nm, src=src:
                              q.dma_start(out=dbg[nm], in_=src), 16)
                for h in range(H):
                    for iw, (qlo, qhi) in enumerate(WINS):
                        nq = qhi - qlo
                        klo = 256 * iw
                        w2 = wcount[0] % 2
                        wcount[0] += 1
                        ab = 4 * w2
                        for kc in range(4):
                            sl = kc % 2
                            p.guard("tensor", sc_use[sl])
                            p.run("tensor", TE, lambda t, sl=sl, kc=kc,
                                  klo=klo, h=h, qlo=qlo, nq=nq: t.matmul(
                                      ps_sc[sl][:, 0:nq],
                                      kr[:, klo + 128 * kc:klo + 128 * (kc + 1)],
                                      qr_ao[:, TOWN * h + qlo:
                                            TOWN * h + qlo + nq],
                                      start=True, stop=True))
                            p.guard("scalar", at_use[ab + kc])
                            p.wait("scalar", TE)
                            p.run("scalar", SE, lambda s, sl=sl, kc=kc, nq=nq,
                                  ab=ab: s.activation(
                                      out=attnT[:, 512 * (ab + kc):
                                               512 * (ab + kc) + nq],
                                      in_=ps_sc[sl][:, 0:nq],
                                      func=AF.Sigmoid, scale=SCALE))
                            sc_use[sl] = p.use("se")
                        p.guard("tensor", pw_use[w2])
                        p.guard("tensor", pz_use[w2])
                        for kc in range(4):
                            kt = klo // 128 + kc
                            p.wait("tensor", SE)
                            p.run("tensor", TE, lambda t, kc=kc, kt=kt, nq=nq,
                                  ab=ab, w2=w2: t.matmul(
                                      ps_w[w2][:, 0:nq],
                                      vtok[:, 129 * kt:129 * kt + 128],
                                      attnT[:, 512 * (ab + kc):
                                            512 * (ab + kc) + nq],
                                      start=(kc == 0), stop=(kc == 3),
                                      skip_group_check=True))
                            p.run("tensor", TE, lambda t, kc=kc, kt=kt, nq=nq,
                                  ab=ab, w2=w2: t.matmul(
                                      ps_z[w2][:, 0:nq],
                                      vtok[:, 129 * kt + 128:129 * kt + 129],
                                      attnT[:, 512 * (ab + kc):
                                            512 * (ab + kc) + nq],
                                      start=(kc == 0), stop=(kc == 3),
                                      skip_group_check=True))
                            at_use[ab + kc] = p.use("te")
                        p.wait("scalar", TE)
                        p.guard("scalar", zrow_use[w2])
                        p.run("scalar", SE, lambda s, nq=nq, w2=w2:
                              s.activation(out=zrow[:, 512 * w2:512 * w2 + nq],
                                           in_=ps_z[w2][:, 0:nq],
                                           func=AF.Copy, bias=0.0, scale=1.0))
                        pz_use[w2] = p.use("se")
                        p.wait("vector", SE)
                        p.run("vector", VE, lambda v, nq=nq, w2=w2:
                              v.reciprocal(out=zbcr[:, 512 * w2:512 * w2 + nq],
                                           in_=zrow[:, 512 * w2:512 * w2 + nq]))
                        zrow_use[w2] = p.use("ve")
                        p.wait("gpsimd", VE)
                        p.run("gpsimd", LV, lambda q, nq=nq, w2=w2:
                              q.dma_start(out=z_scr[w2:w2 + 1, 0:nq],
                                          in_=zbcr[:, 512 * w2:512 * w2 + nq]), 16)
                        p.wait("gpsimd", LV)
                        p.run("gpsimd", LV, lambda q, nq=nq, w2=w2:
                              q.dma_start(
                                  out=zbc[:, 512 * w2:512 * w2 + nq],
                                  in_=z_scr[w2:w2 + 1, 0:nq]
                                  .broadcast_to([128, nq])), 16)
                        p.wait("vector", LV)
                        p.wait("vector", TE)
                        p.run("vector", VE, lambda v, nq=nq, iw=iw, w2=w2:
                              v.tensor_tensor(out=wn[:, WOFF[iw]:WOFF[iw] + nq],
                                              in0=ps_w[w2][:, 0:nq],
                                              in1=zbc[:, 512 * w2:512 * w2 + nq],
                                              op=ALU.mult))
                        pw_use[w2] = p.use("ve")
                    for bb, (iwa, offa, iwb) in enumerate(BLENDS):
                        dst = TOWN * h + 256 * bb
                        sb_ = 512 * (bb % 2)
                        p.guard("vector", bt_use[bb % 2])
                        p.run("vector", VE, lambda v, iwa=iwa, offa=offa,
                              bb=bb, sb_=sb_: v.tensor_tensor(
                                  out=btmp[:, sb_:sb_ + 256],
                                  in0=wn[:, WOFF[iwa] + offa:
                                         WOFF[iwa] + offa + 256],
                                  in1=omac[:, 256 * bb:256 * (bb + 1)],
                                  op=ALU.mult))
                        p.run("vector", VE, lambda v, iwb=iwb, bb=bb, sb_=sb_:
                              v.tensor_tensor(
                                  out=btmp[:, sb_ + 256:sb_ + 512],
                                  in0=wn[:, WOFF[iwb]:WOFF[iwb] + 256],
                                  in1=alpc[:, 256 * bb:256 * (bb + 1)],
                                  op=ALU.mult))
                        p.wait("gpsimd", VE)
                        p.run("gpsimd", GE, lambda g, dst=dst, sb_=sb_:
                              g.tensor_tensor(
                                  out=qr_ao[:, dst:dst + 256],
                                  in0=btmp[:, sb_:sb_ + 256],
                                  in1=btmp[:, sb_ + 256:sb_ + 512], op=ALU.add))
                        bt_use[bb % 2] = p.use("ge")
                    if debug and h == 0:
                        p.wait("sync", VE)
                        p.run("sync", OUT, lambda q: q.dma_start(
                            out=dbg["d_wn"], in_=wn[:]), 16)

        p.barrier()
        with ExitStack() as c4:
            e4 = c4.enter_context
            wsl = e4(nc.sbuf_tensor("wsl", [128, NG * 256], BF16))
            wosl = e4(nc.sbuf_tensor("wosl", [128, H * 256], BF16))
            zs_sl = e4(nc.sbuf_tensor("zs_sl", [NG, 256], F32))
            gsl = e4(nc.sbuf_tensor("gsl", [128, 2 * TOWN], F32))
            dbuf = e4(nc.sbuf_tensor("dbuf", [128, 2 * 512], F32))
            xpc = e4(nc.sbuf_tensor("xpc", [128, 2 * 256], F32))
            outp = e4(nc.sbuf_tensor("outp", [128, 2 * 256], F32))
            ps_f = [e4(nc.psum_tensor(f"ps_f{i}", [128, 512], F32))
                    for i in range(4)]
            ps_dt = [e4(nc.psum_tensor(f"ps_dt{i}", [128, 128], F32))
                     for i in range(2)]
            wsl_use = None
            wosl_use = None
            zs_use = None
            psf_use = [None] * 4
            gsl_use = None
            dbuf_use = None
            dt_use = [None, None]
            xp_use = [None, None]
            out_n = [0]
            p.wait("sync", LV)
            p.wait("scalar", LV)
            p.wait("tensor", VE)
            p.wait("tensor", GE)
            if debug:
                p.wait("sync", VE)
                p.run("sync", OUT, lambda q: q.dma_start(
                    out=dbg["d_ao"], in_=qr_ao[:]), 16)
                p.run("sync", OUT, lambda q: q.dma_start(
                    out=dbg["d_zs"], in_=zs_dram), 16)
                p.run("sync", OUT, lambda q: q.dma_start(
                    out=dbg["d_w"], in_=w_dram[0:256, :]), 16)
            for o16 in range(16):
                jlo = 256 * o16
                p.guard("sync", wsl_use)
                for g in range(NG):
                    p.run("sync", WSA if g < 16 else WSB, lambda q, g=g,
                          jlo=jlo: q.dma_start(
                              out=wsl[:, 256 * g:256 * (g + 1)],
                              in_=w_dram[128 * g:128 * (g + 1),
                                         jlo:jlo + 256]), 16)
                p.guard("gpsimd", wosl_use)
                for i in range(H):
                    p.run("gpsimd", WOA if i < 16 else WOB, lambda q, i=i,
                          jlo=jlo: q.dma_start(
                              out=wosl[:, 256 * i:256 * (i + 1)],
                              in_=woT[128 * i:128 * (i + 1),
                                      jlo:jlo + 256]), 16)
                p.guard("scalar", zs_use)
                p.run("scalar", LC, lambda q, jlo=jlo: q.dma_start(
                    out=zs_sl[:], in_=zs_dram[:, jlo:jlo + 256]), 16)
                for k in range(4):
                    p.guard("tensor", psf_use[k])
                p.wait("tensor", LC)
                for g in range(NG):
                    if g == 0:
                        p.wait("tensor", WSA)
                    elif g == 16:
                        p.wait("tensor", WSB)
                    for jc in range(2):
                        for st in range(2):
                            p.run("tensor", TE, lambda t, g=g, jc=jc, st=st:
                                  t.matmul(ps_f[2 * jc + st][:],
                                           wsl[:, 256 * g + 128 * jc:
                                               256 * g + 128 * (jc + 1)],
                                           xn_own[:, TOWN * g + 512 * st:
                                                  TOWN * g + 512 * (st + 1)],
                                           start=(g == 0), stop=False,
                                           skip_group_check=True))
                wsl_use = p.use("te")
                for jc in range(2):
                    for st in range(2):
                        p.run("tensor", TE, lambda t, jc=jc, st=st: t.matmul(
                            ps_f[2 * jc + st][:],
                            zs_sl[:, 128 * jc:128 * (jc + 1)],
                            csb[:, 512 * st:512 * (st + 1)],
                            start=False, stop=True, skip_group_check=True))
                        p.guard("scalar", gsl_use)
                        p.wait("scalar", TE)
                        p.run("scalar", SE, lambda s, jc=jc, st=st, o16=o16:
                              s.activation(
                                  out=gsl[:, TOWN * jc + 512 * st:
                                          TOWN * jc + 512 * (st + 1)],
                                  in_=ps_f[2 * jc + st][:], func=AF.Silu,
                                  bias=biasc[:, 2 * o16 + jc:2 * o16 + jc + 1],
                                  scale=1.0))
                        psf_use[2 * jc + st] = p.use("se")
                zs_use = p.use("te")
                if debug and o16 == 0:
                    p.wait("sync", SE)
                    p.run("sync", OUT, lambda q: q.dma_start(
                        out=dbg["d_gsl"], in_=gsl[:]), 16)
                for k in range(4):
                    p.guard("tensor", psf_use[k])
                for i in range(H):
                    if i == 0:
                        p.wait("tensor", WOA)
                    elif i == 16:
                        p.wait("tensor", WOB)
                    for oc in range(2):
                        for st in range(2):
                            p.run("tensor", TE, lambda t, i=i, oc=oc, st=st:
                                  t.matmul(ps_f[2 * oc + st][:],
                                           wosl[:, 256 * i + 128 * oc:
                                                256 * i + 128 * (oc + 1)],
                                           qr_ao[:, TOWN * i + 512 * st:
                                                 TOWN * i + 512 * (st + 1)],
                                           start=(i == 0), stop=(i == H - 1),
                                           skip_group_check=True))
                wosl_use = p.use("te")
                for st in range(2):
                    p.guard("vector", dbuf_use)
                    p.wait("vector", TE)
                    for oc in range(2):
                        p.run("vector", VE, lambda v, oc=oc, st=st:
                              v.tensor_tensor(
                                  out=dbuf[:, 512 * oc:512 * (oc + 1)],
                                  in0=ps_f[2 * oc + st][:],
                                  in1=gsl[:, TOWN * oc + 512 * st:
                                          TOWN * oc + 512 * (st + 1)],
                                  op=ALU.mult))
                        psf_use[2 * oc + st] = p.use("ve")
                    gsl_use = p.use("ve")
                    p.wait("tensor", VE)
                    if debug and o16 == 0 and st == 0:
                        p.wait("sync", VE)
                        p.run("sync", OUT, lambda q: q.dma_start(
                            out=dbg["d_dbuf"], in_=dbuf[:]), 16)
                    for ts_ in range(4):
                        trow = 512 * st + 128 * ts_
                        xsl = ts_ % 2
                        p.guard("scalar", xp_use[xsl])
                        p.run("scalar", LC, lambda q, xsl=xsl, trow=trow,
                              jlo=jlo: q.dma_start(
                                  out=xpc[:, 256 * xsl:256 * (xsl + 1)],
                                  in_=x_ctx[OOFF + trow:OOFF + trow + 128,
                                            jlo:jlo + 256]), 16)
                        osl = ts_ % 2
                        if out_n[0] >= 2:
                            p.wait("vector", OUT, 16 * (out_n[0] - 1))
                        for oc in range(2):
                            dsl = oc % 2
                            p.guard("tensor", dt_use[dsl])
                            p.run("tensor", TE, lambda t, dsl=dsl, oc=oc,
                                  ts_=ts_: t.transpose(
                                      ps_dt[dsl][:],
                                      dbuf[:, 512 * oc + 128 * ts_:
                                           512 * oc + 128 * (ts_ + 1)],
                                      identf[:]))
                            p.wait("vector", TE)
                            p.wait("vector", LC)
                            p.run("vector", VE, lambda v, dsl=dsl, oc=oc,
                                  osl=osl, xsl=xsl: v.tensor_tensor(
                                      out=outp[:, 256 * osl + 128 * oc:
                                               256 * osl + 128 * (oc + 1)],
                                      in0=ps_dt[dsl][:],
                                      in1=xpc[:, 256 * xsl + 128 * oc:
                                              256 * xsl + 128 * (oc + 1)],
                                      op=ALU.add))
                            dt_use[dsl] = p.use("ve")
                        dbuf_use = p.use("te")
                        xp_use[xsl] = p.use("ve")
                        p.wait("scalar", VE)
                        p.run("scalar", OUT, lambda q, osl=osl, trow=trow,
                              jlo=jlo: q.dma_start(
                                  out=out_d[trow:trow + 128, jlo:jlo + 256],
                                  in_=outp[:, 256 * osl:256 * (osl + 1)]), 16)
                        out_n[0] += 1

        p.wait("sync", OUT, 16 * out_n[0])
        p.wait("sync", LV)

        blk = e(nc.Block())

        @blk.sync
        def _(eng):
            p.replay("sync", eng)

        @blk.tensor
        def _(eng):
            p.replay("tensor", eng)

        @blk.vector
        def _(eng):
            p.replay("vector", eng)

        @blk.scalar
        def _(eng):
            p.replay("scalar", eng)

        @blk.gpsimd
        def _(eng):
            p.replay("gpsimd", eng)

    return nc


def _rope_cache_np():
    inv_freq = 1.0 / 10000.0 ** (np.arange(0, 128, 2, dtype=np.float32) / 128)
    t = np.arange(SEQ, dtype=np.float32)
    f = np.outer(t, inv_freq)
    cos = np.zeros((SEQ, 128), dtype=np.float32)
    sin = np.zeros((SEQ, 128), dtype=np.float32)
    cos[:, 0::2] = np.cos(f)
    cos[:, 1::2] = np.cos(f)
    sin[:, 0::2] = np.sin(f)
    sin[:, 1::2] = np.sin(f)
    return cos, sin


_NC_CACHE = {}


def prep_in_maps(L0_attn_x, L0_attn_Wq, L0_attn_Wk, L0_attn_Wv, L0_attn_Wo,
                 L0_ffn_qweight, L0_ffn_qzeros, L0_ffn_scales, L0_ffn_bias):
    x = L0_attn_x
    wqT = np.ascontiguousarray(L0_attn_Wq.T)
    woT = np.ascontiguousarray(L0_attn_Wo.T)
    wkT = np.ascontiguousarray(L0_attn_Wk.T)
    wvT = np.ascontiguousarray(L0_attn_Wv.T)
    scl = L0_ffn_scales.astype(np.float32) \
        if L0_ffn_scales.dtype == np.float16 else L0_ffn_scales
    bias = L0_ffn_bias.astype(np.float32) \
        if L0_ffn_bias.dtype == np.float16 else L0_ffn_bias
    bias_t = np.ascontiguousarray(bias.reshape(32, 128).T)
    cos, sin = _rope_cache_np()
    cosT = np.ascontiguousarray(cos.T)
    sinT = np.ascontiguousarray(sin.T)
    zpad = np.zeros((128, 256), dtype=np.float32)
    a = np.linspace(0.0, 1.0, 256, dtype=np.float32)
    alp_std = np.broadcast_to(a, (128, 1024 // 4))
    alp4 = np.tile(alp_std, (1, 4)).astype(np.float32)
    oma4 = np.tile(np.broadcast_to((1.0 - a).astype(np.float32), (128, 256)),
                   (1, 4)).astype(np.float32)
    alp4_even = alp4.copy()
    oma4_even = oma4.copy()
    alp4_even[:, 0:256] = 1.0
    oma4_even[:, 0:256] = 0.0
    shf = (4 * (np.arange(128, dtype=np.int32) % 8)).reshape(128, 1)
    idb = np.eye(128, dtype=ml_dtypes.bfloat16)
    idf = np.eye(128, dtype=np.float32)
    vmk_even = np.ones((128, NT), dtype=ml_dtypes.bfloat16)
    vmk_even[:, 0:2] = 0
    vmk_odd = np.ones((128, NT), dtype=ml_dtypes.bfloat16)
    vmk_odd[:, 10:12] = 0

    xdt = x.dtype
    zrow = np.zeros((256, DM), dtype=xdt)
    in_maps = []
    for c in range(8):
        b, half = c // 2, c % 2
        if half == 0:
            x_ctx = np.concatenate([zrow, x[b, 0:1280, :]], axis=0)
            cosc = np.concatenate([zpad, cosT[:, 0:1280]], axis=1)
            sinc = np.concatenate([zpad, sinT[:, 0:1280]], axis=1)
            vmk, al, om = vmk_even, alp4_even, oma4_even
        else:
            x_ctx = np.concatenate([x[b, 768:2048, :], zrow], axis=0)
            cosc = np.concatenate([cosT[:, 768:2048], zpad], axis=1)
            sinc = np.concatenate([sinT[:, 768:2048], zpad], axis=1)
            vmk, al, om = vmk_odd, alp4, oma4
        in_maps.append({
            "x_ctx": np.ascontiguousarray(x_ctx),
            "wqT": wqT, "woT": woT, "wkT": wkT, "wvT": wvT,
            "qw": L0_ffn_qweight, "qz": L0_ffn_qzeros, "scl": scl,
            "bias_t": bias_t, "cosf": np.ascontiguousarray(cosc),
            "sinf": np.ascontiguousarray(sinc), "alp4": al, "oma4": om,
            "shf": shf, "idb": idb, "idf": idf, "vmk": vmk,
        })
    return in_maps


def kernel(**inputs):
    in_maps = prep_in_maps(**inputs)
    if not _NC_CACHE:
        _NC_CACHE[0] = build_nc()
    nc = _NC_CACHE[0]
    res = run_bass_kernel_spmd(nc, in_maps, core_ids=list(range(8)))
    shards = [res.results[c]["out_s"] for c in range(8)]
    out = np.empty((4, SEQ, DM), dtype=shards[0].dtype)
    for c in range(8):
        b, half = c // 2, c % 2
        out[b, 1024 * half:1024 * (half + 1), :] = shards[c]
    return out


# version 10
# speedup vs baseline: 1.0181x; 1.0181x over previous
import numpy as np
import ml_dtypes
import concourse.bass as bass
import concourse.mybir as mybir
from concourse.bass_utils import run_bass_kernel_spmd
from contextlib import ExitStack

F32 = mybir.dt.float32
BF16 = mybir.dt.bfloat16
I32 = mybir.dt.int32
AF = mybir.ActivationFunctionType
ALU = mybir.AluOpType
AX = mybir.AxisListType

H = 32
DM = 4096
SEQ = 2048
TOWN = 1024
CTX = 1536
OOFF = 256
NG = 32
NT = 12
SCALE = 1.0 / float(np.sqrt(np.float32(128)))
ENGINES = ("sync", "tensor", "vector", "scalar", "gpsimd")

WINS = [(0, 256), (0, 512), (256, 768), (512, 1024), (768, 1024)]
WOFF = [0, 256, 768, 1280, 1792]
WNCOLS = 2048
BLENDS = [(0, 0, 1), (1, 256, 2), (2, 256, 3), (3, 256, 4)]


class Plan:

    def __init__(self):
        self.events = []
        self.ctr = {}

    def sem(self, nc, name, ctx):
        s = ctx.enter_context(nc.semaphore(name))
        self.ctr[name] = [s, 0]
        return name

    def on(self, engine, fn):
        self.events.append((engine, fn))

    def wait(self, engine, semname, val=None):
        s, n = self.ctr[semname]
        v = n if val is None else val
        if v <= 0:
            return
        self.events.append((engine, lambda eng, s=s, v=v: eng.wait_ge(s, v)))

    def guard(self, engine, use):
        if use is not None:
            self.wait(engine, use[0], use[1])

    def cur(self, semname):
        return self.ctr[semname][1]

    def use(self, semname):
        return (semname, self.ctr[semname][1])

    def run(self, engine, semname, fn, amt=1):
        s, _ = self.ctr[semname]
        self.events.append((engine, lambda eng, s=s, fn=fn, amt=amt:
                            fn(eng).then_inc(s, amt)))
        self.ctr[semname][1] += amt
        return self.ctr[semname][1]

    def barrier(self):
        for eng in ENGINES:
            for semname in list(self.ctr):
                self.wait(eng, semname)

    def replay(self, engine_name, eng):
        for e, fn in self.events:
            if e == engine_name:
                fn(eng)


def build_nc(debug=False):
    nc = bass.Bass("TRN2", target_bir_lowering=False)

    def di(name, shape, dt):
        return nc.dram_tensor(name, shape, dt, kind="ExternalInput").ap()

    x_ctx = di("x_ctx", [CTX, DM], F32)
    wqT = di("wqT", [DM, DM], F32)
    woT = di("woT", [DM, DM], F32)
    wkT = di("wkT", [DM, 128], F32)
    wvT = di("wvT", [DM, 128], F32)
    qw = di("qw", [DM // 8, DM], I32)
    qzin = di("qz", [NG, DM // 8], I32)
    scl = di("scl", [NG, DM], F32)
    bias_t = di("bias_t", [128, 32], F32)
    cosf = di("cosf", [128, CTX], F32)
    sinf = di("sinf", [128, CTX], F32)
    alp4 = di("alp4", [128, 1024], F32)
    oma4 = di("oma4", [128, 1024], F32)
    shf = di("shf", [128, 1], I32)
    idb = di("idb", [128, 128], BF16)
    idf = di("idf", [128, 128], F32)
    vmk = di("vmk", [128, NT], BF16)
    out_d = nc.dram_tensor("out_s", [TOWN, DM], F32, kind="ExternalOutput").ap()
    if debug:
        dbg = {n: nc.dram_tensor(n, s, dt, kind="ExternalOutput").ap()
               for n, s, dt in [("d_xn", [128, H * TOWN], BF16),
                                ("d_kr", [128, CTX], BF16),
                                ("d_vtok", [128, NT * 129], BF16),
                                ("d_cs", [NG, TOWN], F32),
                                ("d_qr", [128, H * TOWN], BF16),
                                ("d_ao", [128, H * TOWN], BF16),
                                ("d_wn", [128, 2048], F32),
                                ("d_zs", [NG, DM], F32),
                                ("d_w", [256, DM], BF16),
                                ("d_gsl", [128, 2 * TOWN], F32),
                                ("d_dbuf", [128, 1024], F32)]}
    z_scr = nc.dram_tensor("z_scr", [2, 512], F32, kind="Internal").ap()
    w_dram = nc.dram_tensor("w_dram", [DM, DM], BF16, kind="Internal").ap()
    zs_dram = nc.dram_tensor("zs_dram", [NG, DM], F32, kind="Internal").ap()

    p = Plan()
    with ExitStack() as ctx:
        e = ctx.enter_context
        LS = p.sem(nc, "ls", ctx)
        LG = p.sem(nc, "lg", ctx)
        LC = p.sem(nc, "lc", ctx)
        LV = p.sem(nc, "lv", ctx)
        TE = p.sem(nc, "te", ctx)
        VE = p.sem(nc, "ve", ctx)
        SE = p.sem(nc, "se", ctx)
        GE = p.sem(nc, "ge", ctx)
        OUT = p.sem(nc, "outq", ctx)
        WSA = p.sem(nc, "wsa", ctx)
        WSB = p.sem(nc, "wsb", ctx)
        WOA = p.sem(nc, "woa", ctx)
        WOB = p.sem(nc, "wob", ctx)
        WQA = p.sem(nc, "wqa", ctx)
        WQB = p.sem(nc, "wqb", ctx)

        xn_own = e(nc.sbuf_tensor("xn_own", [128, H * TOWN], BF16))
        qr_ao = e(nc.sbuf_tensor("qr_ao", [128, H * TOWN], BF16))
        csb = e(nc.sbuf_tensor("csb", [NG, TOWN], F32))
        biasc = e(nc.sbuf_tensor("biasc", [128, 32], F32))
        identf = e(nc.sbuf_tensor("identf", [128, 128], F32))
        shfc = e(nc.sbuf_tensor("shfc", [128, 1], I32))
        stats = [e(nc.sbuf_tensor(f"st{i}", [128, 1], F32)) for i in range(6)]
        consts = e(nc.sbuf_tensor("consts", [128, 3], F32))
        nc.const_aps.register(F32, 1e-5, consts[:, 0:1])
        nc.const_aps.register(F32, -8.0, consts[:, 1:2])
        nc.const_aps.register(F32, 0.0, consts[:, 2:3])
        p.run("vector", VE, lambda v: v.memset(consts[:, 0:1], 1e-5))
        p.run("vector", VE, lambda v: v.memset(consts[:, 1:2], -8.0))
        p.run("vector", VE, lambda v: v.memset(consts[:, 2:3], 0.0))
        rp_a = e(nc.sbuf_tensor("rp_a", [128, 512], F32))
        rp_b = e(nc.sbuf_tensor("rp_b", [128, 512], F32))
        rp_c = e(nc.sbuf_tensor("rp_c", [128, 512], F32))

        def ld(dst, src):
            p.run("sync", LS, lambda q, d=dst, s=src: q.dma_start(out=d, in_=s), 16)

        ld(biasc[:], bias_t)
        ld(identf[:], idf)
        ld(shfc[:], shf)

        rp_use = [None]

        def rope(ps, dst, dst_col, width, cosc, sinc, pos):
            p.guard("vector", rp_use[0])
            p.run("vector", VE, lambda v: v.tensor_tensor(
                out=rp_a[0:64, 0:width], in0=ps[64:128, 0:width],
                in1=sinc[0:64, pos:pos + width], op=ALU.mult))
            p.run("vector", VE, lambda v: v.tensor_tensor(
                out=rp_b[64:128, 0:width], in0=ps[0:64, 0:width],
                in1=sinc[64:128, pos:pos + width], op=ALU.mult))
            p.run("vector", VE, lambda v: v.tensor_tensor(
                out=rp_c[:, 0:width], in0=ps[:, 0:width],
                in1=cosc[:, pos:pos + width], op=ALU.mult))
            p.wait("gpsimd", VE)
            p.run("gpsimd", GE, lambda g: g.tensor_tensor(
                out=dst[0:64, dst_col:dst_col + width], in0=rp_c[0:64, 0:width],
                in1=rp_a[0:64, 0:width], op=ALU.subtract))
            p.run("gpsimd", GE, lambda g: g.tensor_tensor(
                out=dst[64:128, dst_col:dst_col + width],
                in0=rp_c[64:128, 0:width], in1=rp_b[64:128, 0:width],
                op=ALU.add))
            rp_use[0] = p.use("ge")

        with ExitStack() as ca:
            ea = ca.enter_context
            kr = ea(nc.sbuf_tensor("kr", [128, CTX], BF16))
            vtok = ea(nc.sbuf_tensor("vtok", [128, NT * 129], BF16))
            cosc = ea(nc.sbuf_tensor("cosc", [128, CTX], F32))
            sinc = ea(nc.sbuf_tensor("sinc", [128, CTX], F32))
            identb = ea(nc.sbuf_tensor("identb", [128, 128], BF16))
            ld(cosc[:], cosf)
            ld(sinc[:], sinf)
            ld(identb[:], idb)
            p.run("vector", VE, lambda v: v.memset(vtok[:], 0.0))
            p.wait("vector", LS)
            p.wait("sync", VE)
            p.run("sync", LV, lambda q: q.dma_start(
                out=vtok[:, 128::129], in_=vmk,
                allow_slow_non_contiguous=True), 16)

            with ExitStack() as c0:
                e0 = c0.enter_context
                qz_sb = e0(nc.sbuf_tensor("qz_sb", [NG, DM // 8], I32))
                ziq = e0(nc.sbuf_tensor("ziq", [NG, 1024], I32))
                zmq = e0(nc.sbuf_tensor("zmq", [NG, 1024], F32))
                zsq = e0(nc.sbuf_tensor("zsq", [NG, 1024], F32))
                sclq = e0(nc.sbuf_tensor("sclq", [NG, 1024], F32))
                ld(qz_sb[:], qzin)
                for q4 in range(4):
                    p.run("sync", LS, lambda q, q4=q4: q.dma_start(
                        out=sclq[:], in_=scl[:, 1024 * q4:1024 * (q4 + 1)]), 16)
                    p.wait("vector", LS)
                    for k in range(8):
                        p.run("vector", VE, lambda v, k=k, q4=q4: v.tensor_scalar(
                            out=ziq[:, k::8], in0=qz_sb[:, 128 * q4:128 * (q4 + 1)],
                            scalar1=4 * k, scalar2=15,
                            op0=ALU.logical_shift_right, op1=ALU.bitwise_and))
                    p.wait("scalar", VE)
                    p.run("scalar", SE, lambda s: s.activation(
                        out=zmq[:], in_=ziq[:], func=AF.Copy,
                        bias=-8.0, scale=1.0))
                    p.wait("vector", SE)
                    p.wait("vector", LV)
                    p.run("vector", VE, lambda v: v.scalar_tensor_tensor(
                        out=zsq[:], in0=zmq[:], scalar=-1.0, in1=sclq[:],
                        op0=ALU.mult, op1=ALU.mult))
                    p.wait("sync", VE)
                    p.run("sync", LV, lambda q, q4=q4: q.dma_start(
                        out=zs_dram[:, 1024 * q4:1024 * (q4 + 1)], in_=zsq[:]), 16)
                    p.wait("sync", LV)

            p.barrier()
            with ExitStack() as c0:
                e0 = c0.enter_context
                wk_sb = e0(nc.sbuf_tensor("wk_sb", [128, H * 128], BF16))
                bmat = e0(nc.sbuf_tensor("bmat", [128, 32 * 32], BF16))
                p.run("vector", VE, lambda v: v.memset(bmat[:], 0.0))
                for g in range(NG):
                    p.run("vector", VE, lambda v, g=g: v.memset(
                        bmat[:, 32 * g + g:32 * g + g + 1], 1.0))
                wv_sb = e0(nc.sbuf_tensor("wv_sb", [128, H * 128], BF16))
                vf = e0(nc.sbuf_tensor("vf", [128, CTX], BF16))
                xt = e0(nc.sbuf_tensor("xt", [128, DM], F32))
                ps_t = [e0(nc.psum_tensor(f"ps_t{i}", [128, 128], BF16))
                        for i in range(2)]
                ps_k = e0(nc.psum_tensor("ps_k", [128, 128], F32))
                ps_v = e0(nc.psum_tensor("ps_v", [128, 128], F32))
                ps_cs = e0(nc.psum_tensor("ps_cs", [NG, TOWN], F32))
                xnt = qr_ao[:, 0:DM]
                hfm = qr_ao[:, DM:2 * DM]
                sqtrash = qr_ao[:, 2 * DM:3 * DM]

                for i in range(H):
                    p.run("gpsimd", LG, lambda q, i=i: q.dma_start(
                        out=wk_sb[:, 128 * i:128 * (i + 1)],
                        in_=wkT[128 * i:128 * (i + 1), :]), 16)
                    p.run("gpsimd", LG, lambda q, i=i: q.dma_start(
                        out=wv_sb[:, 128 * i:128 * (i + 1)],
                        in_=wvT[128 * i:128 * (i + 1), :]), 16)

                sums = e0(nc.sbuf_tensor("sums", [128, NT], F32))
                sqs = e0(nc.sbuf_tensor("sqs", [128, NT], F32))
                meanw = e0(nc.sbuf_tensor("meanw", [128, NT], F32))
                m2w = e0(nc.sbuf_tensor("m2w", [128, NT], F32))
                msqw = e0(nc.sbuf_tensor("msqw", [128, NT], F32))
                varw = e0(nc.sbuf_tensor("varw", [128, NT], F32))
                sqvw = e0(nc.sbuf_tensor("sqvw", [128, NT], F32))
                rstdw = e0(nc.sbuf_tensor("rstdw", [128, NT], F32))
                nmrw = e0(nc.sbuf_tensor("nmrw", [128, NT], F32))
                xa_use = None
                p.wait("tensor", LS)
                for n in range(NT):
                    crow = 128 * n
                    p.guard("sync", xa_use)
                    p.run("sync", LS, lambda q, crow=crow: q.dma_start(
                        out=xt[:], in_=x_ctx[crow:crow + 128, :]), 16)
                    p.wait("vector", LS)
                    p.run("vector", VE, lambda v, n=n: v.reduce_sum(
                        out=sums[:, n:n + 1], in_=xt[:], axis=AX.X))
                    p.wait("scalar", LS)
                    p.run("scalar", SE, lambda s, n=n: s.activation(
                        out=sqtrash, in_=xt[:], func=AF.Square,
                        accum_out=sqs[:, n:n + 1]))
                    xa_use = p.use("se")
                    p.wait("sync", VE)
                p.wait("vector", SE)
                p.run("vector", VE, lambda v: v.tensor_scalar(
                    out=msqw[:], in0=sqs[:], scalar1=1.0 / DM,
                    scalar2=None, op0=ALU.mult))
                p.wait("scalar", VE)
                p.run("scalar", SE, lambda s: s.activation(
                    out=meanw[:], in_=sums[:], func=AF.Copy,
                    bias=0.0, scale=-1.0 / DM))
                p.wait("vector", SE)
                p.run("vector", VE, lambda v: v.tensor_tensor(
                    out=m2w[:], in0=meanw[:], in1=meanw[:], op=ALU.mult))
                p.wait("gpsimd", VE)
                p.run("gpsimd", GE, lambda g: g.tensor_tensor(
                    out=varw[:], in0=msqw[:], in1=m2w[:], op=ALU.subtract))
                p.wait("scalar", GE)
                p.run("scalar", SE, lambda s: s.activation(
                    out=sqvw[:], in_=varw[:], func=AF.Sqrt,
                    bias=1e-5, scale=1.0))
                p.wait("vector", SE)
                p.run("vector", VE, lambda v: v.reciprocal(
                    out=rstdw[:], in_=sqvw[:]))
                p.wait("gpsimd", VE)
                p.run("gpsimd", GE, lambda g: g.tensor_tensor(
                    out=nmrw[:], in0=meanw[:], in1=rstdw[:], op=ALU.mult))
                xt_use = None
                xnt_use = None
                tp_use = [None, None]
                kv_started = False
                p.wait("scalar", GE)
                p.wait("scalar", VE)
                for n in range(NT):
                    crow = 128 * n
                    is_own = 2 <= n < 10
                    p.guard("sync", xt_use)
                    p.run("sync", LS, lambda q, crow=crow: q.dma_start(
                        out=xt[:], in_=x_ctx[crow:crow + 128, :]), 16)
                    p.wait("scalar", LS)
                    p.guard("scalar", xnt_use)
                    p.run("scalar", SE, lambda s, n=n: s.activation(
                        out=xnt, in_=xt[:], func=AF.Identity,
                        bias=nmrw[:, n:n + 1], scale=rstdw[:, n:n + 1]))
                    xt_use = p.use("se")
                    p.wait("tensor", SE)
                    for i in range(H):
                        tsl = i % 2
                        p.guard("tensor", tp_use[tsl])
                        p.run("tensor", TE, lambda t, i=i, tsl=tsl: t.transpose(
                            ps_t[tsl][:], xnt[:, 128 * i:128 * (i + 1)],
                            identb[:]))
                        p.wait("vector", TE)
                        if is_own:
                            dc = TOWN * i + 128 * (n - 2)
                            p.run("vector", VE, lambda v, tsl=tsl, dc=dc:
                                  v.tensor_copy(out=xn_own[:, dc:dc + 128],
                                                in_=ps_t[tsl][:]))
                        else:
                            p.run("vector", VE, lambda v, tsl=tsl, i=i:
                                  v.tensor_copy(out=hfm[:, 128 * i:128 * (i + 1)],
                                                in_=ps_t[tsl][:]))
                        tp_use[tsl] = p.use("ve")
                    xnt_use = p.use("te")
                    if not kv_started:
                        p.wait("tensor", LG)
                        kv_started = True
                    p.wait("tensor", VE)
                    for which, wmat, pps in (("k", wk_sb, ps_k),
                                             ("v", wv_sb, ps_v)):
                        for i in range(H):
                            if is_own:
                                rhs_ap = xn_own[:, TOWN * i + 128 * (n - 2):
                                                TOWN * i + 128 * (n - 2) + 128]
                            else:
                                rhs_ap = hfm[:, 128 * i:128 * (i + 1)]
                            p.run("tensor", TE, lambda t, wmat=wmat, pps=pps,
                                  i=i, rhs_ap=rhs_ap: t.matmul(
                                      pps[:], wmat[:, 128 * i:128 * (i + 1)],
                                      rhs_ap, start=(i == 0), stop=(i == H - 1)))
                        p.wait("vector", TE)
                        if which == "k":
                            rope(ps_k, kr, crow, 128, cosc, sinc, crow)
                        else:
                            p.run("vector", VE, lambda v, crow=crow:
                                  v.tensor_copy(out=vf[:, crow:crow + 128],
                                                in_=ps_v[:]))
                p.wait("tensor", VE)
                for kt in range(NT):
                    tsl = kt % 2
                    p.guard("tensor", tp_use[tsl])
                    p.run("tensor", TE, lambda t, kt=kt, tsl=tsl: t.transpose(
                        ps_t[tsl][:], vf[:, 128 * kt:128 * (kt + 1)], identb[:]))
                    p.wait("vector", TE)
                    p.run("vector", VE, lambda v, kt=kt, tsl=tsl: v.tensor_copy(
                        out=vtok[:, 129 * kt:129 * kt + 128], in_=ps_t[tsl][:]))
                    tp_use[tsl] = p.use("ve")
                for st in range(2):
                    for g in range(NG):
                        p.run("tensor", TE, lambda t, st=st, g=g: t.matmul(
                            ps_cs[:, 512 * st:512 * (st + 1)],
                            bmat[:, 32 * g:32 * (g + 1)],
                            xn_own[:, TOWN * g + 512 * st:
                                   TOWN * g + 512 * (st + 1)],
                            start=(g == 0), stop=(g == NG - 1)))
                p.wait("vector", TE)
                p.run("vector", VE, lambda v: v.tensor_copy(
                    out=csb[:], in_=ps_cs[:]))

            p.barrier()
            with ExitStack() as c1:
                e1 = c1.enter_context
                wq_sl = e1(nc.sbuf_tensor("wq_sl", [128, H * 256], BF16))
                ps_q = [e1(nc.psum_tensor(f"ps_qq{i}", [128, 512], F32))
                        for i in range(8)]
                qraw = e1(nc.sbuf_tensor("qraw", [128, 2 * 1024], I32))
                qint = e1(nc.sbuf_tensor("qint", [128, 2 * 1024], I32))
                qm8f = e1(nc.sbuf_tensor("qm8f", [128, 2 * 1024], BF16))
                sbc = e1(nc.sbuf_tensor("sbc", [128, 2 * 1024], F32))
                wtl = e1(nc.sbuf_tensor("wtl", [128, 2 * 1024], BF16))
                qr_use = [None, None]
                qi_use = [None, None]
                qm_use = [None, None]
                sb_use = [None, None]
                wt_use = [None, None]
                for it in range(128):
                    g, q4 = divmod(it, 4)
                    sl = it % 2
                    c0_, c1_ = 1024 * sl, 1024 * (sl + 1)
                    jlo = 1024 * q4
                    qq = "sync" if (it % 2 == 0) else "gpsimd"
                    qsem = LS if qq == "sync" else LG
                    p.guard(qq, qr_use[sl])
                    p.run(qq, qsem, lambda q, sl=sl, g=g, jlo=jlo,
                          c0_=c0_, c1_=c1_: q.dma_start(
                              out=qraw[:, c0_:c1_],
                              in_=qw[16 * g:16 * (g + 1), jlo:jlo + 1024]
                              .unsqueeze(1).broadcast_to([16, 8, 1024])), 16)
                    p.guard("scalar", sb_use[sl])
                    p.run("scalar", LC, lambda q, sl=sl, g=g, jlo=jlo,
                          c0_=c0_, c1_=c1_: q.dma_start(
                              out=sbc[:, c0_:c1_],
                              in_=scl[g:g + 1, jlo:jlo + 1024]
                              .broadcast_to([128, 1024])), 16)
                    p.wait("vector", LS if qq == "sync" else LG)
                    p.guard("vector", qi_use[sl])
                    p.run("vector", VE, lambda v, c0_=c0_, c1_=c1_:
                          v.tensor_scalar(out=qint[:, c0_:c1_],
                                          in0=qraw[:, c0_:c1_],
                                          scalar1=shfc[:, 0:1], scalar2=15,
                                          op0=ALU.logical_shift_right,
                                          op1=ALU.bitwise_and))
                    qr_use[sl] = p.use("ve")
                    p.wait("scalar", VE)
                    p.guard("scalar", qm_use[sl])
                    p.run("scalar", SE, lambda s, c0_=c0_, c1_=c1_:
                          s.activation(out=qm8f[:, c0_:c1_],
                                       in_=qint[:, c0_:c1_], func=AF.Copy,
                                       bias=-8.0, scale=1.0))
                    qi_use[sl] = p.use("se")
                    p.wait("vector", SE)
                    p.wait("vector", LC)
                    p.guard("vector", wt_use[sl])
                    p.run("vector", VE, lambda v, c0_=c0_, c1_=c1_:
                          v.tensor_tensor(out=wtl[:, c0_:c1_],
                                          in0=qm8f[:, c0_:c1_],
                                          in1=sbc[:, c0_:c1_], op=ALU.mult))
                    qm_use[sl] = p.use("ve")
                    sb_use[sl] = p.use("ve")
                    p.wait("gpsimd", VE)
                    p.run("gpsimd", LV, lambda q, g=g, jlo=jlo, c0_=c0_,
                          c1_=c1_: q.dma_start(
                              out=w_dram[128 * g:128 * (g + 1), jlo:jlo + 1024],
                              in_=wtl[:, c0_:c1_]), 16)
                    wt_use[sl] = ("lv", p.cur("lv"))

                psq_use = [None] * 8
                wq_use = None
                for o16 in range(16):
                    p.guard("gpsimd", wq_use)
                    wq_cnt = []
                    for i in range(H):
                        p.run("gpsimd", LG, lambda q, i=i, o16=o16: q.dma_start(
                            out=wq_sl[:, 256 * i:256 * (i + 1)],
                            in_=wqT[128 * i:128 * (i + 1),
                                    256 * o16:256 * (o16 + 1)]), 16)
                        wq_cnt.append(p.cur("lg"))
                    base = 4 * (o16 % 2)
                    for k in range(4):
                        p.guard("tensor", psq_use[base + k])
                    p.wait("tensor", LG)
                    for i in range(H):
                        for h2 in range(2):
                            for st in range(2):
                                p.run("tensor", TE, lambda t, i=i, h2=h2,
                                      st=st, base=base: t.matmul(
                                          ps_q[base + 2 * h2 + st][:],
                                          wq_sl[:, 256 * i + 128 * h2:
                                                256 * i + 128 * (h2 + 1)],
                                          xn_own[:, TOWN * i + 512 * st:
                                                 TOWN * i + 512 * (st + 1)],
                                          start=(i == 0), stop=(i == H - 1),
                                          skip_group_check=True))
                    wq_use = p.use("te")
                    p.wait("vector", TE)
                    for h2 in range(2):
                        h = 2 * o16 + h2
                        for st in range(2):
                            rope(ps_q[base + 2 * h2 + st], qr_ao,
                                 TOWN * h + 512 * st, 512, cosc, sinc,
                                 OOFF + 512 * st)
                            psq_use[base + 2 * h2 + st] = p.use("ve")

            p.barrier()
            with ExitStack() as c3:
                e3 = c3.enter_context
                attnT = e3(nc.sbuf_tensor("attnT", [128, 8 * 512], BF16))
                alpc = e3(nc.sbuf_tensor("alpc", [128, 1024], F32))
                omac = e3(nc.sbuf_tensor("omac", [128, 1024], F32))
                ld(alpc[:], alp4)
                ld(omac[:], oma4)
                p.wait("vector", LS)
                wn = e3(nc.sbuf_tensor("wn", [128, WNCOLS], F32))
                zbc = e3(nc.sbuf_tensor("zbc", [128, 2 * 512], F32))
                zbcr = e3(nc.sbuf_tensor("zbcr", [1, 1024], F32))
                zrow = e3(nc.sbuf_tensor("zrow", [1, 1024], F32))
                btmp = e3(nc.sbuf_tensor("btmp", [128, 1024], F32))
                bt_use = [None, None]
                ps_sc = [e3(nc.psum_tensor(f"ps_sc{i}", [128, 512], F32))
                         for i in range(2)]
                ps_w = [e3(nc.psum_tensor(f"ps_w{i}", [128, 512], F32))
                        for i in range(2)]
                ps_z = [e3(nc.psum_tensor(f"ps_z{i}", [1, 512], F32))
                        for i in range(2)]
                sc_use = [None, None]
                at_use = [None] * 8
                pw_use = [None, None]
                pz_use = [None, None]
                zrow_use = [None, None]
                wcount = [0]
                p.wait("tensor", VE)
                p.wait("tensor", GE)
                p.wait("tensor", LV, 16)
                if debug:
                    p.wait("sync", VE)
                    p.wait("sync", LV)
                    for nm, src in [("d_xn", xn_own[:]), ("d_kr", kr[:]),
                                    ("d_vtok", vtok[:]), ("d_cs", csb[:]),
                                    ("d_qr", qr_ao[:])]:
                        p.run("sync", OUT, lambda q, nm=nm, src=src:
                              q.dma_start(out=dbg[nm], in_=src), 16)
                for h in range(H):
                    for iw, (qlo, qhi) in enumerate(WINS):
                        nq = qhi - qlo
                        klo = 256 * iw
                        w2 = wcount[0] % 2
                        wcount[0] += 1
                        ab = 4 * w2
                        for kc in range(4):
                            sl = kc % 2
                            p.guard("tensor", sc_use[sl])
                            p.run("tensor", TE, lambda t, sl=sl, kc=kc,
                                  klo=klo, h=h, qlo=qlo, nq=nq: t.matmul(
                                      ps_sc[sl][:, 0:nq],
                                      kr[:, klo + 128 * kc:klo + 128 * (kc + 1)],
                                      qr_ao[:, TOWN * h + qlo:
                                            TOWN * h + qlo + nq],
                                      start=True, stop=True))
                            p.guard("scalar", at_use[ab + kc])
                            p.wait("scalar", TE)
                            p.run("scalar", SE, lambda s, sl=sl, kc=kc, nq=nq,
                                  ab=ab: s.activation(
                                      out=attnT[:, 512 * (ab + kc):
                                               512 * (ab + kc) + nq],
                                      in_=ps_sc[sl][:, 0:nq],
                                      func=AF.Sigmoid, scale=SCALE))
                            sc_use[sl] = p.use("se")
                        p.guard("tensor", pw_use[w2])
                        p.guard("tensor", pz_use[w2])
                        for kc in range(4):
                            kt = klo // 128 + kc
                            p.wait("tensor", SE)
                            p.run("tensor", TE, lambda t, kc=kc, kt=kt, nq=nq,
                                  ab=ab, w2=w2: t.matmul(
                                      ps_w[w2][:, 0:nq],
                                      vtok[:, 129 * kt:129 * kt + 128],
                                      attnT[:, 512 * (ab + kc):
                                            512 * (ab + kc) + nq],
                                      start=(kc == 0), stop=(kc == 3),
                                      skip_group_check=True))
                            p.run("tensor", TE, lambda t, kc=kc, kt=kt, nq=nq,
                                  ab=ab, w2=w2: t.matmul(
                                      ps_z[w2][:, 0:nq],
                                      vtok[:, 129 * kt + 128:129 * kt + 129],
                                      attnT[:, 512 * (ab + kc):
                                            512 * (ab + kc) + nq],
                                      start=(kc == 0), stop=(kc == 3),
                                      skip_group_check=True))
                            at_use[ab + kc] = p.use("te")
                        p.wait("scalar", TE)
                        p.guard("scalar", zrow_use[w2])
                        p.run("scalar", SE, lambda s, nq=nq, w2=w2:
                              s.activation(out=zrow[:, 512 * w2:512 * w2 + nq],
                                           in_=ps_z[w2][:, 0:nq],
                                           func=AF.Copy, bias=0.0, scale=1.0))
                        pz_use[w2] = p.use("se")
                        p.wait("vector", SE)
                        p.run("vector", VE, lambda v, nq=nq, w2=w2:
                              v.reciprocal(out=zbcr[:, 512 * w2:512 * w2 + nq],
                                           in_=zrow[:, 512 * w2:512 * w2 + nq]))
                        zrow_use[w2] = p.use("ve")
                        p.wait("gpsimd", VE)
                        p.run("gpsimd", LV, lambda q, nq=nq, w2=w2:
                              q.dma_start(out=z_scr[w2:w2 + 1, 0:nq],
                                          in_=zbcr[:, 512 * w2:512 * w2 + nq]), 16)
                        p.wait("gpsimd", LV)
                        p.run("gpsimd", LV, lambda q, nq=nq, w2=w2:
                              q.dma_start(
                                  out=zbc[:, 512 * w2:512 * w2 + nq],
                                  in_=z_scr[w2:w2 + 1, 0:nq]
                                  .broadcast_to([128, nq])), 16)
                        p.wait("vector", LV)
                        p.wait("vector", TE)
                        p.run("vector", VE, lambda v, nq=nq, iw=iw, w2=w2:
                              v.tensor_tensor(out=wn[:, WOFF[iw]:WOFF[iw] + nq],
                                              in0=ps_w[w2][:, 0:nq],
                                              in1=zbc[:, 512 * w2:512 * w2 + nq],
                                              op=ALU.mult))
                        pw_use[w2] = p.use("ve")
                    for bb, (iwa, offa, iwb) in enumerate(BLENDS):
                        dst = TOWN * h + 256 * bb
                        sb_ = 512 * (bb % 2)
                        p.guard("vector", bt_use[bb % 2])
                        p.run("vector", VE, lambda v, iwa=iwa, offa=offa,
                              bb=bb, sb_=sb_: v.tensor_tensor(
                                  out=btmp[:, sb_:sb_ + 256],
                                  in0=wn[:, WOFF[iwa] + offa:
                                         WOFF[iwa] + offa + 256],
                                  in1=omac[:, 256 * bb:256 * (bb + 1)],
                                  op=ALU.mult))
                        p.run("vector", VE, lambda v, iwb=iwb, bb=bb, sb_=sb_:
                              v.tensor_tensor(
                                  out=btmp[:, sb_ + 256:sb_ + 512],
                                  in0=wn[:, WOFF[iwb]:WOFF[iwb] + 256],
                                  in1=alpc[:, 256 * bb:256 * (bb + 1)],
                                  op=ALU.mult))
                        p.wait("gpsimd", VE)
                        p.run("gpsimd", GE, lambda g, dst=dst, sb_=sb_:
                              g.tensor_tensor(
                                  out=qr_ao[:, dst:dst + 256],
                                  in0=btmp[:, sb_:sb_ + 256],
                                  in1=btmp[:, sb_ + 256:sb_ + 512], op=ALU.add))
                        bt_use[bb % 2] = p.use("ge")
                    if debug and h == 0:
                        p.wait("sync", VE)
                        p.run("sync", OUT, lambda q: q.dma_start(
                            out=dbg["d_wn"], in_=wn[:]), 16)

        p.barrier()
        with ExitStack() as c4:
            e4 = c4.enter_context
            wsl = e4(nc.sbuf_tensor("wsl", [128, NG * 256], BF16))
            wosl = e4(nc.sbuf_tensor("wosl", [128, H * 256], BF16))
            zs_sl = e4(nc.sbuf_tensor("zs_sl", [NG, 256], F32))
            gsl = e4(nc.sbuf_tensor("gsl", [128, 2 * TOWN], F32))
            dbuf = e4(nc.sbuf_tensor("dbuf", [128, 2 * 512], F32))
            xpc = e4(nc.sbuf_tensor("xpc", [128, 2 * 256], F32))
            outp = e4(nc.sbuf_tensor("outp", [128, 2 * 256], F32))
            ps_f = [e4(nc.psum_tensor(f"ps_f{i}", [128, 512], F32))
                    for i in range(4)]
            ps_dt = [e4(nc.psum_tensor(f"ps_dt{i}", [128, 128], F32))
                     for i in range(2)]
            wsl_use = None
            wosl_use = None
            zs_use = None
            psf_use = [None] * 4
            gsl_use = None
            dbuf_use = None
            dt_use = [None, None]
            xp_use = [None, None]
            out_n = [0]
            p.wait("sync", LV)
            p.wait("scalar", LV)
            p.wait("tensor", VE)
            p.wait("tensor", GE)
            if debug:
                p.wait("sync", VE)
                p.run("sync", OUT, lambda q: q.dma_start(
                    out=dbg["d_ao"], in_=qr_ao[:]), 16)
                p.run("sync", OUT, lambda q: q.dma_start(
                    out=dbg["d_zs"], in_=zs_dram), 16)
                p.run("sync", OUT, lambda q: q.dma_start(
                    out=dbg["d_w"], in_=w_dram[0:256, :]), 16)
            for o16 in range(16):
                jlo = 256 * o16
                p.guard("sync", wsl_use)
                for g in range(NG):
                    p.run("sync", WSA if g < 16 else WSB, lambda q, g=g,
                          jlo=jlo: q.dma_start(
                              out=wsl[:, 256 * g:256 * (g + 1)],
                              in_=w_dram[128 * g:128 * (g + 1),
                                         jlo:jlo + 256]), 16)
                p.guard("gpsimd", wosl_use)
                for i in range(H):
                    p.run("gpsimd", WOA if i < 16 else WOB, lambda q, i=i,
                          jlo=jlo: q.dma_start(
                              out=wosl[:, 256 * i:256 * (i + 1)],
                              in_=woT[128 * i:128 * (i + 1),
                                      jlo:jlo + 256]), 16)
                p.guard("scalar", zs_use)
                p.run("scalar", LC, lambda q, jlo=jlo: q.dma_start(
                    out=zs_sl[:], in_=zs_dram[:, jlo:jlo + 256]), 16)
                for k in range(4):
                    p.guard("tensor", psf_use[k])
                p.wait("tensor", LC)
                for g in range(NG):
                    if g == 0:
                        p.wait("tensor", WSA)
                    elif g == 16:
                        p.wait("tensor", WSB)
                    for jc in range(2):
                        for st in range(2):
                            p.run("tensor", TE, lambda t, g=g, jc=jc, st=st:
                                  t.matmul(ps_f[2 * jc + st][:],
                                           wsl[:, 256 * g + 128 * jc:
                                               256 * g + 128 * (jc + 1)],
                                           xn_own[:, TOWN * g + 512 * st:
                                                  TOWN * g + 512 * (st + 1)],
                                           start=(g == 0), stop=False,
                                           skip_group_check=True))
                wsl_use = p.use("te")
                for jc in range(2):
                    for st in range(2):
                        p.run("tensor", TE, lambda t, jc=jc, st=st: t.matmul(
                            ps_f[2 * jc + st][:],
                            zs_sl[:, 128 * jc:128 * (jc + 1)],
                            csb[:, 512 * st:512 * (st + 1)],
                            start=False, stop=True, skip_group_check=True))
                        p.guard("scalar", gsl_use)
                        p.wait("scalar", TE)
                        p.run("scalar", SE, lambda s, jc=jc, st=st, o16=o16:
                              s.activation(
                                  out=gsl[:, TOWN * jc + 512 * st:
                                          TOWN * jc + 512 * (st + 1)],
                                  in_=ps_f[2 * jc + st][:], func=AF.Silu,
                                  bias=biasc[:, 2 * o16 + jc:2 * o16 + jc + 1],
                                  scale=1.0))
                        psf_use[2 * jc + st] = p.use("se")
                zs_use = p.use("te")
                if debug and o16 == 0:
                    p.wait("sync", SE)
                    p.run("sync", OUT, lambda q: q.dma_start(
                        out=dbg["d_gsl"], in_=gsl[:]), 16)
                for k in range(4):
                    p.guard("tensor", psf_use[k])
                for i in range(H):
                    if i == 0:
                        p.wait("tensor", WOA)
                    elif i == 16:
                        p.wait("tensor", WOB)
                    for oc in range(2):
                        for st in range(2):
                            p.run("tensor", TE, lambda t, i=i, oc=oc, st=st:
                                  t.matmul(ps_f[2 * oc + st][:],
                                           wosl[:, 256 * i + 128 * oc:
                                                256 * i + 128 * (oc + 1)],
                                           qr_ao[:, TOWN * i + 512 * st:
                                                 TOWN * i + 512 * (st + 1)],
                                           start=(i == 0), stop=(i == H - 1),
                                           skip_group_check=True))
                wosl_use = p.use("te")
                for st in range(2):
                    p.guard("vector", dbuf_use)
                    p.wait("vector", TE)
                    for oc in range(2):
                        p.run("vector", VE, lambda v, oc=oc, st=st:
                              v.tensor_tensor(
                                  out=dbuf[:, 512 * oc:512 * (oc + 1)],
                                  in0=ps_f[2 * oc + st][:],
                                  in1=gsl[:, TOWN * oc + 512 * st:
                                          TOWN * oc + 512 * (st + 1)],
                                  op=ALU.mult))
                        psf_use[2 * oc + st] = p.use("ve")
                    gsl_use = p.use("ve")
                    p.wait("tensor", VE)
                    if debug and o16 == 0 and st == 0:
                        p.wait("sync", VE)
                        p.run("sync", OUT, lambda q: q.dma_start(
                            out=dbg["d_dbuf"], in_=dbuf[:]), 16)
                    for ts_ in range(4):
                        trow = 512 * st + 128 * ts_
                        xsl = ts_ % 2
                        p.guard("scalar", xp_use[xsl])
                        p.run("scalar", LC, lambda q, xsl=xsl, trow=trow,
                              jlo=jlo: q.dma_start(
                                  out=xpc[:, 256 * xsl:256 * (xsl + 1)],
                                  in_=x_ctx[OOFF + trow:OOFF + trow + 128,
                                            jlo:jlo + 256]), 16)
                        osl = ts_ % 2
                        if out_n[0] >= 2:
                            p.wait("vector", OUT, 16 * (out_n[0] - 1))
                        for oc in range(2):
                            dsl = oc % 2
                            p.guard("tensor", dt_use[dsl])
                            p.run("tensor", TE, lambda t, dsl=dsl, oc=oc,
                                  ts_=ts_: t.transpose(
                                      ps_dt[dsl][:],
                                      dbuf[:, 512 * oc + 128 * ts_:
                                           512 * oc + 128 * (ts_ + 1)],
                                      identf[:]))
                            p.wait("vector", TE)
                            p.wait("vector", LC)
                            p.run("vector", VE, lambda v, dsl=dsl, oc=oc,
                                  osl=osl, xsl=xsl: v.tensor_tensor(
                                      out=outp[:, 256 * osl + 128 * oc:
                                               256 * osl + 128 * (oc + 1)],
                                      in0=ps_dt[dsl][:],
                                      in1=xpc[:, 256 * xsl + 128 * oc:
                                              256 * xsl + 128 * (oc + 1)],
                                      op=ALU.add))
                            dt_use[dsl] = p.use("ve")
                        dbuf_use = p.use("te")
                        xp_use[xsl] = p.use("ve")
                        p.wait("scalar", VE)
                        p.run("scalar", OUT, lambda q, osl=osl, trow=trow,
                              jlo=jlo: q.dma_start(
                                  out=out_d[trow:trow + 128, jlo:jlo + 256],
                                  in_=outp[:, 256 * osl:256 * (osl + 1)]), 16)
                        out_n[0] += 1

        p.wait("sync", OUT, 16 * out_n[0])
        p.wait("sync", LV)

        blk = e(nc.Block())

        @blk.sync
        def _(eng):
            p.replay("sync", eng)

        @blk.tensor
        def _(eng):
            p.replay("tensor", eng)

        @blk.vector
        def _(eng):
            p.replay("vector", eng)

        @blk.scalar
        def _(eng):
            p.replay("scalar", eng)

        @blk.gpsimd
        def _(eng):
            p.replay("gpsimd", eng)

    return nc


def _rope_cache_np():
    inv_freq = 1.0 / 10000.0 ** (np.arange(0, 128, 2, dtype=np.float32) / 128)
    t = np.arange(SEQ, dtype=np.float32)
    f = np.outer(t, inv_freq)
    cos = np.zeros((SEQ, 128), dtype=np.float32)
    sin = np.zeros((SEQ, 128), dtype=np.float32)
    cos[:, 0::2] = np.cos(f)
    cos[:, 1::2] = np.cos(f)
    sin[:, 0::2] = np.sin(f)
    sin[:, 1::2] = np.sin(f)
    return cos, sin


_NC_CACHE = {}


def prep_in_maps(L0_attn_x, L0_attn_Wq, L0_attn_Wk, L0_attn_Wv, L0_attn_Wo,
                 L0_ffn_qweight, L0_ffn_qzeros, L0_ffn_scales, L0_ffn_bias):
    x = L0_attn_x
    wqT = np.ascontiguousarray(L0_attn_Wq.T)
    woT = np.ascontiguousarray(L0_attn_Wo.T)
    wkT = np.ascontiguousarray(L0_attn_Wk.T)
    wvT = np.ascontiguousarray(L0_attn_Wv.T)
    scl = L0_ffn_scales.astype(np.float32) \
        if L0_ffn_scales.dtype == np.float16 else L0_ffn_scales
    bias = L0_ffn_bias.astype(np.float32) \
        if L0_ffn_bias.dtype == np.float16 else L0_ffn_bias
    bias_t = np.ascontiguousarray(bias.reshape(32, 128).T)
    cos, sin = _rope_cache_np()
    cosT = np.ascontiguousarray(cos.T)
    sinT = np.ascontiguousarray(sin.T)
    zpad = np.zeros((128, 256), dtype=np.float32)
    a = np.linspace(0.0, 1.0, 256, dtype=np.float32)
    alp_std = np.broadcast_to(a, (128, 1024 // 4))
    alp4 = np.tile(alp_std, (1, 4)).astype(np.float32)
    oma4 = np.tile(np.broadcast_to((1.0 - a).astype(np.float32), (128, 256)),
                   (1, 4)).astype(np.float32)
    alp4_even = alp4.copy()
    oma4_even = oma4.copy()
    alp4_even[:, 0:256] = 1.0
    oma4_even[:, 0:256] = 0.0
    shf = (4 * (np.arange(128, dtype=np.int32) % 8)).reshape(128, 1)
    idb = np.eye(128, dtype=ml_dtypes.bfloat16)
    idf = np.eye(128, dtype=np.float32)
    vmk_even = np.ones((128, NT), dtype=ml_dtypes.bfloat16)
    vmk_even[:, 0:2] = 0
    vmk_odd = np.ones((128, NT), dtype=ml_dtypes.bfloat16)
    vmk_odd[:, 10:12] = 0

    xdt = x.dtype
    zrow = np.zeros((256, DM), dtype=xdt)
    in_maps = []
    for c in range(8):
        b, half = c // 2, c % 2
        if half == 0:
            x_ctx = np.concatenate([zrow, x[b, 0:1280, :]], axis=0)
            cosc = np.concatenate([zpad, cosT[:, 0:1280]], axis=1)
            sinc = np.concatenate([zpad, sinT[:, 0:1280]], axis=1)
            vmk, al, om = vmk_even, alp4_even, oma4_even
        else:
            x_ctx = np.concatenate([x[b, 768:2048, :], zrow], axis=0)
            cosc = np.concatenate([cosT[:, 768:2048], zpad], axis=1)
            sinc = np.concatenate([sinT[:, 768:2048], zpad], axis=1)
            vmk, al, om = vmk_odd, alp4, oma4
        in_maps.append({
            "x_ctx": np.ascontiguousarray(x_ctx),
            "wqT": wqT, "woT": woT, "wkT": wkT, "wvT": wvT,
            "qw": L0_ffn_qweight, "qz": L0_ffn_qzeros, "scl": scl,
            "bias_t": bias_t, "cosf": np.ascontiguousarray(cosc),
            "sinf": np.ascontiguousarray(sinc), "alp4": al, "oma4": om,
            "shf": shf, "idb": idb, "idf": idf, "vmk": vmk,
        })
    return in_maps


def kernel(**inputs):
    in_maps = prep_in_maps(**inputs)
    if not _NC_CACHE:
        _NC_CACHE[0] = build_nc()
    nc = _NC_CACHE[0]
    res = run_bass_kernel_spmd(nc, in_maps, core_ids=list(range(8)))
    shards = [res.results[c]["out_s"] for c in range(8)]
    out = np.empty((4, SEQ, DM), dtype=shards[0].dtype)
    for c in range(8):
        b, half = c // 2, c % 2
        out[b, 1024 * half:1024 * (half + 1), :] = shards[c]
    return out
